# version 05
# baseline (speedup 1.0000x reference)
import numpy as np
import concourse.bass as bass
import concourse.mybir as mybir
from concourse.bass_utils import run_bass_kernel_spmd

F32 = mybir.dt.float32
I32 = mybir.dt.int32
AF = mybir.ActivationFunctionType
OP = mybir.AluOpType
AX = mybir.AxisListType.X

ENGINES = ("tensor", "vector", "scalar", "gpsimd", "sync")
GROUPS = [[0, 1, 2, 3], [4, 5, 6, 7]]

H = 4096
D = 1024
T = 1024
NT = 8
INV_H = 1.0 / 4096.0
SM_SCALE = 1.0 / 32.0
EPS = 1e-5


class Prog:

    def __init__(self):
        self.ops = {e: [] for e in ENGINES}
        self.cnt = {e: 0 for e in ENGINES}
        self.is_dma_tok = {e: set() for e in ENGINES}

    def emit(self, eng, method, *args, inc=1, deps=(), **kwargs):
        dl = []
        for d in deps:
            if d is None:
                continue
            de, dv = d
            if dv <= 0:
                continue
            if de == eng and dv not in self.is_dma_tok[de]:
                continue
            dl.append((de, dv))
        self.ops[eng].append(("w", dl))
        self.ops[eng].append(("i", method, args, kwargs, inc))
        self.cnt[eng] += inc
        if inc == 16:
            self.is_dma_tok[eng].add(self.cnt[eng])
        return (eng, self.cnt[eng])

    def barrier(self):
        snap = dict(self.cnt)
        for e in ENGINES:
            dl = [(de, dv) for de, dv in snap.items() if de != e and dv > 0]
            self.ops[e].append(("w", dl))
        return snap

    def replay(self, eng, engobj, sems):
        waited = {}
        for op in self.ops[eng]:
            if op[0] == "w":
                for de, dv in op[1]:
                    if waited.get(de, -1) >= dv:
                        continue
                    engobj.wait_ge(sems[de], dv)
                    waited[de] = dv
            else:
                _, method, args, kwargs, inc = op
                inst = getattr(engobj, method)(*args, **kwargs)
                inst.then_inc(sems[eng], inc)


def build():
    nc = bass.Bass("TRN2", target_bir_lowering=False)
    p = Prog()

    x_own = nc.dram_tensor("x_own", [T, H], F32, kind="ExternalInput").ap()
    wqkv = nc.dram_tensor("wqkv", [H, 3 * D], F32, kind="ExternalInput").ap()
    bqkv_cm = nc.dram_tensor("bqkv_cm", [128, 24], F32, kind="ExternalInput").ap()
    wo = nc.dram_tensor("wo", [D, H], F32, kind="ExternalInput").ap()
    bo_r = nc.dram_tensor("bo_r", [1, H], F32, kind="ExternalInput").ap()
    wffnT = nc.dram_tensor("wffnT", [H, H], F32, kind="ExternalInput").ap()
    bffn_r = nc.dram_tensor("bffn_r", [1, H], F32, kind="ExternalInput").ap()
    atc = nc.dram_tensor("atc", [H, 48], F32, kind="ExternalInput").ap()
    btc = nc.dram_tensor("btc", [48, H], F32, kind="ExternalInput").ap()
    idx_in = nc.dram_tensor("idx_in", [1, 4], I32, kind="ExternalInput").ap()
    bval_in = nc.dram_tensor("bval_in", [1, 1], F32, kind="ExternalInput").ap()
    out_own = nc.dram_tensor("out_own", [T, H], F32, kind="ExternalOutput").ap()
    cc_in = nc.dram_tensor("cc_in", [2 * T, D], F32, kind="Internal")
    cc_out = nc.dram_tensor("cc_out", [8 * T, D], F32, kind="Internal")
    x1_dram = nc.dram_tensor("x1_dram", [T, H], F32, kind="Internal").ap()
    y_dram = nc.dram_tensor("y_dram", [T, H], F32, kind="Internal").ap()
    cc_in_ap = cc_in.ap()
    cc_out_ap = cc_out.ap()

    from contextlib import ExitStack
    es = ExitStack()
    with es:
        A = es.enter_context(nc.sbuf_tensor("A", [128, 32768], F32))
        Bq = es.enter_context(nc.sbuf_tensor("Bq", [128, 8192], F32))
        C = es.enter_context(nc.sbuf_tensor("C", [128, 8192], F32))
        wb = es.enter_context(nc.sbuf_tensor("wb", [128, 2048], F32))
        ident = es.enter_context(nc.sbuf_tensor("ident", [128, 128], F32))
        ones = es.enter_context(nc.sbuf_tensor("ones", [1, 128], F32))
        bq_cm = es.enter_context(nc.sbuf_tensor("bq_cm", [128, 24], F32))
        idx_sb = es.enter_context(nc.sbuf_tensor("idx_sb", [1, 4], I32))
        bv_sb = es.enter_context(nc.sbuf_tensor("bv_sb", [1, 1], F32))
        lw = es.enter_context(nc.sbuf_tensor("lw", [48, 24], F32))
        st1 = es.enter_context(nc.sbuf_tensor("st1", [128, 96], F32))
        st2 = es.enter_context(nc.sbuf_tensor("st2", [128, 64], F32))
        PS = [es.enter_context(nc.psum_tensor("ps%d" % i, [128, 512], F32))
              for i in range(8)]
        sems = {}
        for e in ENGINES:
            sems[e] = es.enter_context(nc.semaphore("sem_" + e))
        block = es.enter_context(nc.Block())

        EPS_AP = st1[:, 95:96]

        def vgap(n=8):
            for i in range(n):
                p.emit("vector", "tensor_single_scalar", st2[:, i:i + 1],
                       st2[:, i:i + 1], 0.0, op=OP.add)

        p.emit("gpsimd", "memset", ident[:], 0.0)
        t_id = p.emit("gpsimd", "affine_select", out=ident[:], in_=ident[:],
                      compare_op=OP.not_equal, fill=1.0, base=0,
                      pattern=[[-1, 128]], channel_multiplier=1)
        t_ones = p.emit("gpsimd", "memset", ones[:], 1.0)
        t_eps = p.emit("gpsimd", "memset", st1[:, 95:96], EPS)
        t_bqcm = p.emit("gpsimd", "dma_start", inc=16, out=bq_cm[:], in_=bqkv_cm)
        t_idx = p.emit("gpsimd", "dma_start", inc=16, out=idx_sb[:], in_=idx_in)
        t_bv = p.emit("gpsimd", "dma_start", inc=16, out=bv_sb[:], in_=bval_in)
        t_io = p.emit("gpsimd", "iota", lw[0:48, 19:20], pattern=[[0, 1]],
                      channel_multiplier=1, allow_small_or_imprecise_dtypes=True)
        t_j1 = p.emit("vector", "tensor_single_scalar", lw[0:48, 20:21],
                      lw[0:48, 19:20], 15.5, op=OP.is_ge, deps=[t_io])
        t_j2 = p.emit("vector", "tensor_single_scalar", lw[0:48, 21:22],
                      lw[0:48, 19:20], 31.5, op=OP.is_ge)
        vgap()
        t_jp = p.emit("vector", "tensor_tensor", lw[0:48, 15:16],
                      lw[0:48, 20:21], lw[0:48, 21:22], op=OP.add)
        t_i2f = p.emit("vector", "tensor_copy", lw[0:1, 0:4], idx_sb[:], deps=[t_idx])
        t_bvc = p.emit("vector", "tensor_copy", lw[0:1, 4:5], bv_sb[:], deps=[t_bv])
        t_bc = p.emit("tensor", "matmul", PS[0][0:48, 0:5], ones[0:1, 0:48],
                      lw[0:1, 0:5], start=True, stop=True,
                      deps=[t_ones, t_i2f, t_bvc])
        t_bce = p.emit("scalar", "activation", lw[0:48, 0:5], PS[0][0:48, 0:5],
                       AF.Copy, bias=0.0, deps=[t_bc])
        t_le = p.emit("vector", "tensor_scalar", lw[0:48, 8:12], lw[0:48, 0:4],
                      lw[0:48, 4:5], None, op0=OP.is_le, deps=[t_bce])
        vgap()
        t_cn = p.emit("vector", "reduce_sum", lw[0:48, 12:13], lw[0:48, 8:12], axis=AX)
        vgap()
        t_sg = p.emit("vector", "tensor_single_scalar", lw[0:48, 13:14],
                      lw[0:48, 12:13], 1.0, op=OP.subtract)
        vgap()
        t_a1 = p.emit("vector", "tensor_single_scalar", lw[0:48, 14:15],
                      lw[0:48, 13:14], 1.0, op=OP.subtract)
        vgap()
        t_a2 = p.emit("vector", "tensor_single_scalar", lw[0:48, 14:15],
                      lw[0:48, 14:15], 0.0, op=OP.max)
        vgap()
        t_a3 = p.emit("vector", "tensor_single_scalar", lw[0:48, 14:15],
                      lw[0:48, 14:15], 2.0, op=OP.min)
        vgap()
        t_eq = p.emit("vector", "tensor_tensor", lw[0:48, 16:17], lw[0:48, 14:15],
                      lw[0:48, 15:16], op=OP.is_equal, deps=[t_jp])
        t_us = p.emit("vector", "tensor_single_scalar", lw[0:48, 17:18],
                      lw[0:48, 13:14], 0.5, op=OP.is_ge)
        vgap()
        t_wv = p.emit("vector", "tensor_tensor", lw[0:48, 18:19], lw[0:48, 16:17],
                      lw[0:48, 17:18], op=OP.mult)
        wvec = lw[0:48, 18:19]

        def ln_norm(src_ap, xm_ap, dump_ap, deps):
            tv0 = p.emit("vector", "reduce_sum", st1[:, 88:89], src_ap,
                         axis=AX, deps=deps)
            ts0 = p.emit("scalar", "activation", st1[:, 89:90], st1[:, 88:89],
                         AF.Copy, bias=0.0, scale=-INV_H, deps=[tv0])
            tv1 = p.emit("vector", "tensor_scalar", xm_ap, src_ap,
                         st1[:, 89:90], None, op0=OP.add, deps=[ts0] + list(deps))
            ts1 = p.emit("scalar", "activation", dump_ap, xm_ap, AF.Square,
                         accum_out=st1[:, 90:91], deps=[tv1])
            tv2 = p.emit("vector", "tensor_copy", st1[:, 91:92], st1[:, 90:91],
                         deps=[ts1])
            ts2 = p.emit("scalar", "activation", st1[:, 92:93], st1[:, 91:92],
                         AF.Sqrt, bias=EPS_AP, scale=INV_H, deps=[tv2, t_eps])
            tv3 = p.emit("vector", "reciprocal", st1[:, 93:94], st1[:, 92:93],
                         deps=[ts2])
            ts3 = p.emit("scalar", "activation", xm_ap, xm_ap, AF.Copy,
                         bias=0.0, scale=st1[:, 93:94], deps=[tv3])
            return ts3

        wslot_free = [None] * 4
        strip_free = [None] * 8
        qkv_done_tok = None
        last_ccin = None
        t_h1_prev = None
        t_tr_prev = None

        for half in range(2):
            prev_half_mm = qkv_done_tok
            ln_toks = []
            for lt in range(4):
                g_tt = half * 4 + lt
                c1 = C[:, 0:4096]
                c2 = C[:, 4096:8192]
                t_x = p.emit("gpsimd", "dma_start", inc=16, out=c1,
                             in_=x_own[g_tt * 128:(g_tt + 1) * 128, :],
                             deps=list(strip_free) + [prev_half_mm, t_h1_prev,
                                                      t_tr_prev])
                t_h1 = ln_norm(c1, c2, c1, deps=[t_x])
                for d4 in range(8):
                    bank = PS[d4 % 8]
                    for c in range(4):
                        d = d4 * 4 + c
                        p.emit("tensor", "transpose", bank[:, c * 128:(c + 1) * 128],
                               c2[:, d * 128:(d + 1) * 128], ident[:],
                               deps=[t_h1, t_id] if (d4 == 0 and c == 0) else [t_h1])
                    t_tr = ("tensor", p.cnt["tensor"])
                    t_ev = p.emit("vector", "tensor_copy",
                                  A[:, lt * 4096 + d4 * 512: lt * 4096 + (d4 + 1) * 512],
                                  bank[:, 0:512], deps=[t_tr, prev_half_mm])
                    ln_toks.append(t_ev)
                t_h1_prev = t_h1
                t_tr_prev = ("tensor", p.cnt["tensor"])
            h1T_ready = ln_toks[-1]
            last_evac = [None] * 4
            last_tr_ev = [None] * 8
            for og in range(6):
                for d in range(32):
                    slot = (og * 32 + d) % 4
                    t_w = p.emit("sync", "dma_start", inc=16,
                                 out=wb[:, slot * 512:(slot + 1) * 512],
                                 in_=wqkv[d * 128:(d + 1) * 128, og * 512:(og + 1) * 512],
                                 deps=[wslot_free[slot]])
                    for lt in range(4):
                        deps = [t_w, h1T_ready]
                        if d == 0:
                            deps.append(last_evac[lt])
                        p.emit("tensor", "matmul", PS[lt][:, 0:512],
                               A[:, lt * 4096 + d * 128: lt * 4096 + (d + 1) * 128],
                               wb[:, slot * 512:(slot + 1) * 512],
                               start=(d == 0), stop=(d == 31), deps=deps)
                    wslot_free[slot] = ("tensor", p.cnt["tensor"])
                t_chain = ("tensor", p.cnt["tensor"])
                for lt in range(4):
                    g_tt = half * 4 + lt
                    sslot = (og * 4 + lt) % 8
                    sap = C[:, sslot * 512:(sslot + 1) * 512]
                    t_ev = p.emit("vector", "tensor_copy", sap, PS[lt][:, 0:512],
                                  deps=[t_chain, strip_free[sslot]])
                    last_evac[lt] = t_ev
                    if og < 4:
                        tb = 4 + lt
                        bank = PS[tb]
                        for c in range(4):
                            p.emit("tensor", "transpose", bank[:, c * 128:(c + 1) * 128],
                                   sap[:, c * 128:(c + 1) * 128], ident[:],
                                   deps=[t_ev, last_tr_ev[tb]] if c == 0 else [t_ev])
                        t_tr = ("tensor", p.cnt["tensor"])
                        if og < 2:
                            tk = None
                            for c in range(4):
                                dd = og * 4 + c
                                tk = p.emit("vector", "tensor_scalar",
                                            Bq[:, g_tt * 1024 + dd * 128: g_tt * 1024 + (dd + 1) * 128],
                                            bank[:, c * 128:(c + 1) * 128],
                                            bq_cm[:, dd:dd + 1], None, op0=OP.add,
                                            deps=[t_tr, t_bqcm])
                            last_tr_ev[tb] = tk
                            strip_free[sslot] = t_tr
                        else:
                            tdma = None
                            tk = None
                            for c in range(4):
                                kd = (og - 2) * 4 + c
                                tk = p.emit("vector", "tensor_scalar",
                                            sap[:, c * 128:(c + 1) * 128],
                                            bank[:, c * 128:(c + 1) * 128],
                                            bq_cm[:, 8 + kd:9 + kd], None, op0=OP.add,
                                            deps=[t_tr, t_bqcm])
                                tdma = p.emit("gpsimd", "dma_start", inc=16,
                                              out=cc_in_ap[kd * 128:(kd + 1) * 128,
                                                           g_tt * 128:(g_tt + 1) * 128],
                                              in_=sap[:, c * 128:(c + 1) * 128],
                                              deps=[tk])
                            last_tr_ev[tb] = tk
                            strip_free[sslot] = tdma
                            last_ccin = tdma
                    else:
                        tdma = p.emit("gpsimd", "dma_start", inc=16,
                                      out=cc_in_ap[D + g_tt * 128: D + (g_tt + 1) * 128,
                                                   (og - 4) * 512:(og - 3) * 512],
                                      in_=sap, deps=[t_ev])
                        strip_free[sslot] = tdma
                        last_ccin = tdma
            qkv_done_tok = ("tensor", p.cnt["tensor"])
        p.barrier()

        t_cc = p.emit("gpsimd", "collective_compute", "AllGather", OP.bypass,
                      replica_groups=GROUPS, ins=[cc_in_ap.opt()],
                      outs=[cc_out_ap.opt()], deps=[last_ccin])
        p.is_dma_tok["gpsimd"].add(t_cc[1])

        ktb = [A[:, 0:4096], A[:, 4096:8192]]
        vvb = [A[:, 8192:12288], A[:, 12288:16384]]
        pslot = [A[:, 16384 + q * 512: 16384 + (q + 1) * 512] for q in range(8)]
        ptslot = [A[:, 20480 + q * 512: 20480 + (q + 1) * 512] for q in range(8)]
        O = C
        t_o0 = p.emit("gpsimd", "memset", C[:], 0.0, deps=[t_cc])
        t_m0 = p.emit("gpsimd", "memset", st1[:, 0:8], 1e30)
        t_l0 = p.emit("gpsimd", "memset", st1[:, 8:16], 0.0)
        kv_used = [None, None]
        ps_free = [None] * 8
        pt_used = [None] * 8
        pt_evac = [None] * 8
        av_free = [None] * 8
        sc_done = [None] * 8
        c_read = [None] * 8
        l_upd = [None] * 8
        for kb in range(8):
            pp = kb % 2
            rb = (kb // 2) * 2048
            cb2 = (kb % 2) * 512
            tk_d = None
            for d in range(8):
                tk_d = p.emit("gpsimd", "dma_start", inc=16,
                              out=ktb[pp][:, d * 512:(d + 1) * 512],
                              in_=cc_out_ap[rb + d * 128: rb + (d + 1) * 128, cb2:cb2 + 512],
                              deps=[t_cc, kv_used[pp]] if d == 0 else [])
            tv_d = None
            for kc in range(4):
                tv_d = p.emit("gpsimd", "dma_start", inc=16,
                              out=vvb[pp][:, kc * 1024:(kc + 1) * 1024],
                              in_=cc_out_ap[rb + D + cb2 + kc * 128: rb + D + cb2 + (kc + 1) * 128, 0:D],
                              deps=[])
            t_s = [None] * 8
            for q in range(8):
                for d in range(8):
                    deps = [tk_d, ps_free[q]] if d == 0 else []
                    p.emit("tensor", "matmul", PS[q][:, 0:512],
                           Bq[:, q * 1024 + d * 128: q * 1024 + (d + 1) * 128],
                           ktb[pp][:, d * 512:(d + 1) * 512],
                           start=(d == 0), stop=(d == 7), deps=deps)
                t_s[q] = ("tensor", p.cnt["tensor"])
            s1 = [None] * 8
            for q in range(8):
                s1[q] = p.emit("vector", "reduce_max", st1[:, 16 + q:17 + q],
                               PS[q][:, 0:512], axis=AX, negate=True,
                               deps=[t_s[q], l_upd[q], sc_done[q]])
            s2 = [None] * 8
            for q in range(8):
                s2[q] = p.emit("scalar", "activation", st1[:, 24 + q:25 + q],
                               st1[:, 16 + q:17 + q], AF.Copy, bias=0.0,
                               scale=SM_SCALE, deps=[s1[q], sc_done[q]])
            s3 = [None] * 8
            for q in range(8):
                s3[q] = p.emit("vector", "tensor_tensor", st1[:, 32 + q:33 + q],
                               st1[:, 24 + q:25 + q], st1[:, q:q + 1], op=OP.min,
                               deps=[s2[q], t_m0, sc_done[q]])
            s4 = [None] * 8
            for q in range(8):
                s4[q] = p.emit("vector", "tensor_tensor", st1[:, 40 + q:41 + q],
                               st1[:, 24 + q:25 + q], st1[:, q:q + 1],
                               op=OP.subtract, deps=[s2[q]])
            for q in range(8):
                p.emit("vector", "tensor_single_scalar", st1[:, 48 + q:49 + q],
                       st1[:, 40 + q:41 + q], 0.0, op=OP.min)
            s5 = [None] * 8
            for q in range(8):
                s5[q] = p.emit("vector", "tensor_single_scalar", st1[:, 48 + q:49 + q],
                               st1[:, 48 + q:49 + q], -87.0, op=OP.max)
            s6 = [None] * 8
            for q in range(8):
                s6[q] = p.emit("scalar", "activation", st1[:, 64 + q:65 + q],
                               st1[:, 48 + q:49 + q], AF.Exp,
                               deps=[s5[q], c_read[q]])
            s7 = [None] * 8
            for q in range(8):
                s7[q] = p.emit("vector", "tensor_copy", st1[:, q:q + 1],
                               st1[:, 32 + q:33 + q], deps=[s6[q]])
            s8 = [None] * 8
            for q in range(8):
                s8[q] = p.emit("scalar", "activation", pslot[q], PS[q][:, 0:512],
                               AF.Exp, bias=st1[:, 32 + q:33 + q], scale=SM_SCALE,
                               accum_out=st1[:, 72 + q:73 + q],
                               deps=[s3[q], pt_used[q]])
                sc_done[q] = s8[q]
            for q in range(8):
                l_upd[q] = p.emit("vector", "scalar_tensor_tensor",
                                  st1[:, 8 + q:9 + q], st1[:, 8 + q:9 + q],
                                  st1[:, 64 + q:65 + q], st1[:, 72 + q:73 + q],
                                  op0=OP.mult, op1=OP.add,
                                  deps=[s6[q], s8[q], t_l0])
            for q in range(8):
                for c in range(4):
                    p.emit("tensor", "transpose", PS[q][:, c * 128:(c + 1) * 128],
                           pslot[q][:, c * 128:(c + 1) * 128], ident[:],
                           deps=[s8[q], pt_evac[q]] if c == 0 else [s8[q]])
                pt_used[q] = ("tensor", p.cnt["tensor"])
                t_pte = p.emit("vector", "tensor_copy", ptslot[q], PS[q][:, 0:512],
                               deps=[pt_used[q]])
                pt_evac[q] = t_pte
                ps_free[q] = t_pte
            for q in range(8):
                b0 = (2 * q) % 8
                for dh in range(2):
                    bank = PS[b0 + dh]
                    for kc in range(4):
                        deps = []
                        if kc == 0:
                            deps = [pt_evac[q], tv_d, av_free[b0 + dh],
                                    ps_free[b0 + dh]]
                        p.emit("tensor", "matmul", bank[:, 0:512],
                               ptslot[q][:, kc * 128:(kc + 1) * 128],
                               vvb[pp][:, kc * 1024 + dh * 512: kc * 1024 + dh * 512 + 512],
                               start=(kc == 0), stop=(kc == 3), deps=deps)
                    t_av = ("tensor", p.cnt["tensor"])
                    t_oe = p.emit("vector", "scalar_tensor_tensor",
                                  O[:, q * 1024 + dh * 512: q * 1024 + dh * 512 + 512],
                                  O[:, q * 1024 + dh * 512: q * 1024 + dh * 512 + 512],
                                  st1[:, 64 + q:65 + q], bank[:, 0:512],
                                  op0=OP.mult, op1=OP.add,
                                  deps=[t_av, t_o0, l_upd[q]])
                    av_free[b0 + dh] = t_oe
                    ps_free[b0 + dh] = t_oe
                    c_read[q] = t_oe
            kv_used[pp] = ("tensor", p.cnt["tensor"])
        p.barrier()

        rl_t = [None] * 8
        for q in range(8):
            rl_t[q] = p.emit("vector", "reciprocal", st1[:, 80 + q:81 + q],
                             st1[:, 8 + q:9 + q], deps=[l_upd[q]])
        on_t = [None] * 8
        for q in range(8):
            on_t[q] = p.emit("vector", "tensor_scalar", C[:, q * 1024:(q + 1) * 1024],
                             C[:, q * 1024:(q + 1) * 1024], st1[:, 80 + q:81 + q],
                             None, op0=OP.mult, deps=[rl_t[q]])
        ot_ev = [None] * 8
        for q in range(8):
            a = (q % 4) * 2
            for g in range(2):
                bank = PS[a + g]
                for c in range(4):
                    dd = g * 4 + c
                    p.emit("tensor", "transpose", bank[:, c * 128:(c + 1) * 128],
                           C[:, q * 1024 + dd * 128: q * 1024 + (dd + 1) * 128],
                           ident[:], deps=[on_t[q], ot_ev[a + g]] if c == 0 else [on_t[q]])
                t_tr = ("tensor", p.cnt["tensor"])
                tk = None
                for c in range(4):
                    dd = g * 4 + c
                    tk = p.emit("vector", "tensor_scalar",
                                Bq[:, dd * 1024 + q * 128: dd * 1024 + (q + 1) * 128],
                                bank[:, c * 128:(c + 1) * 128],
                                bq_cm[:, 16 + dd:17 + dd], None, op0=OP.add,
                                deps=[t_tr])
                ot_ev[a + g] = tk
        ot_done = ("vector", p.cnt["vector"])
        t_bo = p.emit("gpsimd", "dma_start", inc=16, out=C[0:1, 0:4096], in_=bo_r,
                      deps=[("tensor", p.cnt["tensor"])])
        wslot_free = [None] * 4
        z_ev = [None] * 8
        for og in range(8):
            for dd in range(8):
                slot = (og * 8 + dd) % 4
                t_w = p.emit("sync", "dma_start", inc=16,
                             out=wb[:, slot * 512:(slot + 1) * 512],
                             in_=wo[dd * 128:(dd + 1) * 128, og * 512:(og + 1) * 512],
                             deps=[wslot_free[slot]])
                if dd == 0:
                    for q in range(8):
                        p.emit("tensor", "matmul", PS[q][:, 0:512],
                               ones[0:1, 0:128], C[0:1, og * 512:(og + 1) * 512],
                               start=True, stop=False,
                               deps=[t_bo, z_ev[q], ot_done])
                for q in range(8):
                    p.emit("tensor", "matmul", PS[q][:, 0:512],
                           Bq[:, dd * 1024 + q * 128: dd * 1024 + (q + 1) * 128],
                           wb[:, slot * 512:(slot + 1) * 512],
                           start=False, stop=(dd == 7),
                           deps=[t_w] if q == 0 else [])
                wslot_free[slot] = ("tensor", p.cnt["tensor"])
            t_chain = ("tensor", p.cnt["tensor"])
            for q in range(8):
                z_ev[q] = p.emit("vector", "tensor_copy",
                                 A[:, q * 4096 + og * 512: q * 4096 + (og + 1) * 512],
                                 PS[q][:, 0:512], deps=[t_chain])
        p.barrier()

        x1_st = [None] * 8
        for tt in range(8):
            c1 = C[:, 0:4096]
            c2 = C[:, 4096:8192]
            za = A[:, tt * 4096:(tt + 1) * 4096]
            t_x = p.emit("gpsimd", "dma_start", inc=16, out=c1,
                         in_=x_own[tt * 128:(tt + 1) * 128, :],
                         deps=[("scalar", p.cnt["scalar"]),
                               ("vector", p.cnt["vector"]),
                               ("tensor", p.cnt["tensor"])])
            t_t1 = ln_norm(za, c2, Bq[:, 4096:8192], deps=[t_x])
            t_x1 = p.emit("vector", "tensor_tensor", Bq[:, 0:4096], c2, c1,
                          op=OP.add, deps=[t_t1, t_x, x1_st[tt - 1] if tt else None])
            x1_st[tt] = p.emit("gpsimd", "dma_start", inc=16,
                               out=x1_dram[tt * 128:(tt + 1) * 128, :],
                               in_=Bq[:, 0:4096], deps=[t_x1])
            t_h2 = ln_norm(Bq[:, 0:4096], c1, Bq[:, 4096:8192], deps=[t_x1])
            for d4 in range(8):
                bank = PS[d4 % 8]
                for c in range(4):
                    d = d4 * 4 + c
                    p.emit("tensor", "transpose", bank[:, c * 128:(c + 1) * 128],
                           c1[:, d * 128:(d + 1) * 128], ident[:], deps=[t_h2])
                t_tr = ("tensor", p.cnt["tensor"])
                p.emit("vector", "tensor_copy",
                       za[:, d4 * 512:(d4 + 1) * 512], bank[:, 0:512], deps=[t_tr])
        p.barrier()

        t_at = None
        for d in range(32):
            t_at = p.emit("sync", "dma_start", inc=16, out=Bq[:, d * 48:(d + 1) * 48],
                          in_=atc[d * 128:(d + 1) * 128, :])
        t_bt = p.emit("sync", "dma_start", inc=16, out=C[0:48, 0:4096], in_=btc)
        t_bf = p.emit("sync", "dma_start", inc=16, out=Bq[0:1, 3072:7168], in_=bffn_r)
        t_z1 = p.emit("gpsimd", "memset", st2[:, 0:8], 0.0)
        t_z2 = p.emit("gpsimd", "memset", st2[:, 8:16], 0.0)
        rt_ev = [None] * 8
        for tt in range(8):
            for d in range(32):
                p.emit("tensor", "matmul", PS[tt][0:48, 0:128],
                       Bq[:, d * 48:(d + 1) * 48],
                       A[:, tt * 4096 + d * 128: tt * 4096 + (d + 1) * 128],
                       start=(d == 0), stop=(d == 31),
                       deps=[t_at] if d == 0 else [])
            t_ch = ("tensor", p.cnt["tensor"])
            rt_ev[tt] = p.emit("vector", "tensor_scalar",
                               Bq[0:48, 1536 + tt * 128: 1536 + (tt + 1) * 128],
                               PS[tt][0:48, 0:128], wvec, None, op0=OP.mult,
                               deps=[t_ch, t_wv])
        wslot_free = [None] * 4
        y_ev = [None] * 8
        strip_dma = [None] * 8
        strip_sq = [None] * 8
        dump = Bq[:, 2560:3072]
        for og in range(8):
            for d in range(32):
                slot = (og * 32 + d) % 4
                t_w = p.emit("sync", "dma_start", inc=16,
                             out=wb[:, slot * 512:(slot + 1) * 512],
                             in_=wffnT[d * 128:(d + 1) * 128, og * 512:(og + 1) * 512],
                             deps=[wslot_free[slot], t_bf])
                if d == 0:
                    for tt in range(8):
                        p.emit("tensor", "matmul", PS[tt][:, 0:512],
                               ones[0:1, 0:128], Bq[0:1, 3072 + og * 512: 3072 + (og + 1) * 512],
                               start=True, stop=False,
                               deps=[t_bf, y_ev[tt], rt_ev[tt]])
                for tt in range(8):
                    p.emit("tensor", "matmul", PS[tt][:, 0:512],
                           A[:, tt * 4096 + d * 128: tt * 4096 + (d + 1) * 128],
                           wb[:, slot * 512:(slot + 1) * 512],
                           start=False, stop=False, deps=[t_w] if tt == 0 else [])
                wslot_free[slot] = ("tensor", p.cnt["tensor"])
                if d == 31:
                    for tt in range(8):
                        p.emit("tensor", "matmul", PS[tt][:, 0:512],
                               Bq[0:48, 1536 + tt * 128: 1536 + (tt + 1) * 128],
                               C[0:48, og * 512:(og + 1) * 512],
                               start=False, stop=True,
                               deps=[t_bt, rt_ev[tt]] if tt == 0 else [rt_ev[tt]])
            t_chain = ("tensor", p.cnt["tensor"])
            evs = []
            for tt in range(8):
                sap = C[:, 4096 + tt * 512: 4096 + (tt + 1) * 512]
                y_ev[tt] = p.emit("vector", "tensor_copy", sap, PS[tt][:, 0:512],
                                  deps=[t_chain, strip_dma[tt], strip_sq[tt]])
                evs.append(y_ev[tt])
            rs_t = [None] * 8
            for tt in range(8):
                rs_t[tt] = p.emit("vector", "reduce_sum", st2[:, 16 + tt:17 + tt],
                                  C[:, 4096 + tt * 512: 4096 + (tt + 1) * 512],
                                  axis=AX)
            for tt in range(8):
                strip_sq[tt] = p.emit("scalar", "activation", dump,
                                      C[:, 4096 + tt * 512: 4096 + (tt + 1) * 512],
                                      AF.Square, accum_out=st2[:, 24 + tt:25 + tt],
                                      deps=[evs[tt]])
            for tt in range(8):
                strip_dma[tt] = p.emit("gpsimd", "dma_start", inc=16,
                                       out=y_dram[tt * 128:(tt + 1) * 128,
                                                  og * 512:(og + 1) * 512],
                                       in_=C[:, 4096 + tt * 512: 4096 + (tt + 1) * 512],
                                       deps=[evs[tt]])
            for tt in range(8):
                p.emit("vector", "tensor_tensor", st2[:, tt:tt + 1],
                       st2[:, tt:tt + 1], st2[:, 16 + tt:17 + tt], op=OP.add,
                       deps=[t_z1, rs_t[tt]])
            for tt in range(8):
                p.emit("vector", "tensor_tensor", st2[:, 8 + tt:9 + tt],
                       st2[:, 8 + tt:9 + tt], st2[:, 24 + tt:25 + tt], op=OP.add,
                       deps=[t_z2, strip_sq[tt]])
        acc_done = ("vector", p.cnt["vector"])
        for tt in range(8):
            p.emit("vector", "tensor_single_scalar", st2[:, 32 + tt:33 + tt],
                   st2[:, tt:tt + 1], INV_H, op=OP.mult, deps=[acc_done])
        for tt in range(8):
            p.emit("vector", "tensor_single_scalar", st2[:, 40 + tt:41 + tt],
                   st2[:, 8 + tt:9 + tt], INV_H, op=OP.mult)
        for tt in range(8):
            p.emit("vector", "tensor_tensor", st2[:, 48 + tt:49 + tt],
                   st2[:, 32 + tt:33 + tt], st2[:, 32 + tt:33 + tt], op=OP.mult)
        v4_t = [None] * 8
        for tt in range(8):
            v4_t[tt] = p.emit("vector", "tensor_tensor", st2[:, 48 + tt:49 + tt],
                              st2[:, 40 + tt:41 + tt], st2[:, 48 + tt:49 + tt],
                              op=OP.subtract)
        sd_t = [None] * 8
        for tt in range(8):
            sd_t[tt] = p.emit("scalar", "activation", st2[:, 16 + tt:17 + tt],
                              st2[:, 48 + tt:49 + tt], AF.Sqrt, bias=EPS_AP,
                              scale=1.0, deps=[v4_t[tt], t_eps])
        for tt in range(8):
            p.emit("vector", "reciprocal", st2[:, 56 + tt:57 + tt],
                   st2[:, 16 + tt:17 + tt], deps=[sd_t[tt]])
        for tt in range(8):
            p.emit("vector", "tensor_single_scalar", st2[:, 24 + tt:25 + tt],
                   st2[:, 32 + tt:33 + tt], -1.0, op=OP.mult)
        p.barrier()

        out_toks = []
        for tt in range(8):
            c1 = C[:, 0:4096]
            c2 = C[:, 4096:8192]
            aa = A[:, 0:4096]
            dep_prev = out_toks[-1] if out_toks else None
            t_y = p.emit("gpsimd", "dma_start", inc=16, out=c1,
                         in_=y_dram[tt * 128:(tt + 1) * 128, :],
                         deps=[("scalar", p.cnt["scalar"]),
                               ("vector", p.cnt["vector"]), dep_prev])
            t_x1 = p.emit("sync", "dma_start", inc=16, out=c2,
                          in_=x1_dram[tt * 128:(tt + 1) * 128, :],
                          deps=[("vector", p.cnt["vector"])])
            t_s1 = p.emit("vector", "tensor_scalar", aa, c1,
                          st2[:, 24 + tt:25 + tt], None, op0=OP.add,
                          deps=[t_y, dep_prev])
            t_t = p.emit("scalar", "activation", aa, aa, AF.Copy,
                         bias=0.0, scale=st2[:, 56 + tt:57 + tt], deps=[t_s1])
            t_o = p.emit("vector", "tensor_tensor", c1, aa, c2,
                         op=OP.add, deps=[t_t, t_x1])
            out_toks.append(p.emit("gpsimd", "dma_start", inc=16,
                                   out=out_own[tt * 128:(tt + 1) * 128, :],
                                   in_=c1, deps=[t_o]))
        final = p.cnt["gpsimd"]
        p.ops["gpsimd"].append(("w", [("gpsimd", final)]))

        @block.tensor
        def _(eng):
            p.replay("tensor", eng, sems)

        @block.vector
        def _(eng):
            p.replay("vector", eng, sems)

        @block.scalar
        def _(eng):
            p.replay("scalar", eng, sems)

        @block.gpsimd
        def _(eng):
            p.replay("gpsimd", eng, sems)

        @block.sync
        def _(eng):
            p.replay("sync", eng, sems)

    return nc


def kernel(L0_attn_x, L0_attn_Wqkv, L0_attn_bqkv, L0_attn_Wo, L0_attn_bo,
           L0_ffn_weight, L0_ffn_bias, L0_ffn_lora_A, L0_ffn_lora_B,
           L0_ffn_lora_start_indices):
    f = np.float32
    x = np.asarray(L0_attn_x, f)
    wqkv = np.ascontiguousarray(np.asarray(L0_attn_Wqkv, f))
    bqkv_cm = np.ascontiguousarray(np.asarray(L0_attn_bqkv, f).reshape(24, 128).T)
    wo = np.ascontiguousarray(np.asarray(L0_attn_Wo, f))
    bo_r = np.ascontiguousarray(np.asarray(L0_attn_bo, f).reshape(1, H))
    wffnT = np.ascontiguousarray(np.asarray(L0_ffn_weight, f).T)
    bffn_r = np.ascontiguousarray(np.asarray(L0_ffn_bias, f).reshape(1, H))
    atc = np.ascontiguousarray(np.asarray(L0_ffn_lora_A, f).transpose(2, 0, 1).reshape(H, 48))
    btc = np.ascontiguousarray(np.asarray(L0_ffn_lora_B, f).transpose(0, 2, 1).reshape(48, H))
    idx = np.ascontiguousarray(np.asarray(L0_ffn_lora_start_indices, np.int32).reshape(1, 4))

    nc = build()
    in_maps = []
    for c in range(8):
        b, qc = c // 4, c % 4
        in_maps.append({
            "x_own": np.ascontiguousarray(x[b, qc * T:(qc + 1) * T, :]),
            "wqkv": wqkv, "bqkv_cm": bqkv_cm, "wo": wo, "bo_r": bo_r,
            "wffnT": wffnT, "bffn_r": bffn_r, "atc": atc, "btc": btc,
            "idx_in": idx, "bval_in": np.array([[float(b)]], f),
        })
    res = run_bass_kernel_spmd(nc, in_maps, core_ids=list(range(8)))
    out = np.empty((2, 4096, H), f)
    for c in range(8):
        b, qc = c // 4, c % 4
        out[b, qc * T:(qc + 1) * T, :] = np.asarray(res.results[c]["out_own"])
    return out


# version 08 (final)
# speedup vs baseline: 1.0175x; 1.0175x over previous
import numpy as np
import concourse.bass as bass
import concourse.mybir as mybir
from concourse.bass_utils import run_bass_kernel_spmd

F32 = mybir.dt.float32
I32 = mybir.dt.int32
AF = mybir.ActivationFunctionType
OP = mybir.AluOpType
AX = mybir.AxisListType.X

ENGINES = ("tensor", "vector", "scalar", "gpsimd", "sync")
GROUPS = [[0, 1, 2, 3], [4, 5, 6, 7]]

H = 4096
D = 1024
T = 1024
NT = 8
INV_H = 1.0 / 4096.0
SM_SCALE = 1.0 / 32.0
EPS = 1e-5


class Prog:

    def __init__(self):
        self.ops = {e: [] for e in ENGINES}
        self.cnt = {e: 0 for e in ENGINES}
        self.is_dma_tok = {e: set() for e in ENGINES}

    def emit(self, eng, method, *args, inc=1, deps=(), **kwargs):
        dl = []
        for d in deps:
            if d is None:
                continue
            de, dv = d
            if dv <= 0:
                continue
            if de == eng and dv not in self.is_dma_tok[de]:
                continue
            dl.append((de, dv))
        self.ops[eng].append(("w", dl))
        self.ops[eng].append(("i", method, args, kwargs, inc))
        self.cnt[eng] += inc
        if inc == 16:
            self.is_dma_tok[eng].add(self.cnt[eng])
        return (eng, self.cnt[eng])

    def barrier(self):
        snap = dict(self.cnt)
        for e in ENGINES:
            dl = [(de, dv) for de, dv in snap.items() if de != e and dv > 0]
            self.ops[e].append(("w", dl))
        return snap

    def replay(self, eng, engobj, sems):
        waited = {}
        for op in self.ops[eng]:
            if op[0] == "w":
                for de, dv in op[1]:
                    if waited.get(de, -1) >= dv:
                        continue
                    engobj.wait_ge(sems[de], dv)
                    waited[de] = dv
            else:
                _, method, args, kwargs, inc = op
                inst = getattr(engobj, method)(*args, **kwargs)
                inst.then_inc(sems[eng], inc)


def build():
    nc = bass.Bass("TRN2", target_bir_lowering=False)
    p = Prog()

    x_own = nc.dram_tensor("x_own", [T, H], F32, kind="ExternalInput").ap()
    wqkv = nc.dram_tensor("wqkv", [H, 3 * D], F32, kind="ExternalInput").ap()
    bqkv_cm = nc.dram_tensor("bqkv_cm", [128, 24], F32, kind="ExternalInput").ap()
    wo = nc.dram_tensor("wo", [D, H], F32, kind="ExternalInput").ap()
    bo_r = nc.dram_tensor("bo_r", [1, H], F32, kind="ExternalInput").ap()
    wffnT = nc.dram_tensor("wffnT", [H, H], F32, kind="ExternalInput").ap()
    bffn_r = nc.dram_tensor("bffn_r", [1, H], F32, kind="ExternalInput").ap()
    atc = nc.dram_tensor("atc", [H, 48], F32, kind="ExternalInput").ap()
    btc = nc.dram_tensor("btc", [48, H], F32, kind="ExternalInput").ap()
    idx_in = nc.dram_tensor("idx_in", [1, 4], I32, kind="ExternalInput").ap()
    bval_in = nc.dram_tensor("bval_in", [1, 1], F32, kind="ExternalInput").ap()
    out_own = nc.dram_tensor("out_own", [T, H], F32, kind="ExternalOutput").ap()
    cc_in = nc.dram_tensor("cc_in", [2 * T, D], F32, kind="Internal")
    cc_out = nc.dram_tensor("cc_out", [8 * T, D], F32, kind="Internal")
    x1_dram = nc.dram_tensor("x1_dram", [T, H], F32, kind="Internal").ap()
    y_dram = nc.dram_tensor("y_dram", [T, H], F32, kind="Internal").ap()
    cc_in_ap = cc_in.ap()
    cc_out_ap = cc_out.ap()

    from contextlib import ExitStack
    es = ExitStack()
    with es:
        A = es.enter_context(nc.sbuf_tensor("A", [128, 32768], F32))
        Bq = es.enter_context(nc.sbuf_tensor("Bq", [128, 8192], F32))
        C = es.enter_context(nc.sbuf_tensor("C", [128, 8192], F32))
        wb = es.enter_context(nc.sbuf_tensor("wb", [128, 2048], F32))
        ident = es.enter_context(nc.sbuf_tensor("ident", [128, 128], F32))
        ones = es.enter_context(nc.sbuf_tensor("ones", [1, 128], F32))
        bq_cm = es.enter_context(nc.sbuf_tensor("bq_cm", [128, 24], F32))
        idx_sb = es.enter_context(nc.sbuf_tensor("idx_sb", [1, 4], I32))
        bv_sb = es.enter_context(nc.sbuf_tensor("bv_sb", [1, 1], F32))
        lw = es.enter_context(nc.sbuf_tensor("lw", [48, 24], F32))
        st1 = es.enter_context(nc.sbuf_tensor("st1", [128, 96], F32))
        st2 = es.enter_context(nc.sbuf_tensor("st2", [128, 64], F32))
        PS = [es.enter_context(nc.psum_tensor("ps%d" % i, [128, 512], F32))
              for i in range(8)]
        sems = {}
        for e in ENGINES:
            sems[e] = es.enter_context(nc.semaphore("sem_" + e))
        block = es.enter_context(nc.Block())

        EPS_AP = st1[:, 95:96]

        def vgap(n=8):
            for i in range(n):
                p.emit("vector", "tensor_single_scalar", st2[:, i:i + 1],
                       st2[:, i:i + 1], 0.0, op=OP.add)

        p.emit("gpsimd", "memset", ident[:], 0.0)
        t_id = p.emit("gpsimd", "affine_select", out=ident[:], in_=ident[:],
                      compare_op=OP.not_equal, fill=1.0, base=0,
                      pattern=[[-1, 128]], channel_multiplier=1)
        t_ones = p.emit("gpsimd", "memset", ones[:], 1.0)
        t_eps = p.emit("gpsimd", "memset", st1[:, 95:96], EPS)
        t_bqcm = p.emit("gpsimd", "dma_start", inc=16, out=bq_cm[:], in_=bqkv_cm)
        t_idx = p.emit("gpsimd", "dma_start", inc=16, out=idx_sb[:], in_=idx_in)
        t_bv = p.emit("gpsimd", "dma_start", inc=16, out=bv_sb[:], in_=bval_in)
        t_io = p.emit("gpsimd", "iota", lw[0:48, 19:20], pattern=[[0, 1]],
                      channel_multiplier=1, allow_small_or_imprecise_dtypes=True)
        t_j1 = p.emit("vector", "tensor_single_scalar", lw[0:48, 20:21],
                      lw[0:48, 19:20], 15.5, op=OP.is_ge, deps=[t_io])
        t_j2 = p.emit("vector", "tensor_single_scalar", lw[0:48, 21:22],
                      lw[0:48, 19:20], 31.5, op=OP.is_ge)
        vgap()
        t_jp = p.emit("vector", "tensor_tensor", lw[0:48, 15:16],
                      lw[0:48, 20:21], lw[0:48, 21:22], op=OP.add)
        t_i2f = p.emit("vector", "tensor_copy", lw[0:1, 0:4], idx_sb[:], deps=[t_idx])
        t_bvc = p.emit("vector", "tensor_copy", lw[0:1, 4:5], bv_sb[:], deps=[t_bv])
        t_bc = p.emit("tensor", "matmul", PS[0][0:48, 0:5], ones[0:1, 0:48],
                      lw[0:1, 0:5], start=True, stop=True,
                      deps=[t_ones, t_i2f, t_bvc])
        t_bce = p.emit("scalar", "activation", lw[0:48, 0:5], PS[0][0:48, 0:5],
                       AF.Copy, bias=0.0, deps=[t_bc])
        t_le = p.emit("vector", "tensor_scalar", lw[0:48, 8:12], lw[0:48, 0:4],
                      lw[0:48, 4:5], None, op0=OP.is_le, deps=[t_bce])
        vgap()
        t_cn = p.emit("vector", "reduce_sum", lw[0:48, 12:13], lw[0:48, 8:12], axis=AX)
        vgap()
        t_sg = p.emit("vector", "tensor_single_scalar", lw[0:48, 13:14],
                      lw[0:48, 12:13], 1.0, op=OP.subtract)
        vgap()
        t_a1 = p.emit("vector", "tensor_single_scalar", lw[0:48, 14:15],
                      lw[0:48, 13:14], 1.0, op=OP.subtract)
        vgap()
        t_a2 = p.emit("vector", "tensor_single_scalar", lw[0:48, 14:15],
                      lw[0:48, 14:15], 0.0, op=OP.max)
        vgap()
        t_a3 = p.emit("vector", "tensor_single_scalar", lw[0:48, 14:15],
                      lw[0:48, 14:15], 2.0, op=OP.min)
        vgap()
        t_eq = p.emit("vector", "tensor_tensor", lw[0:48, 16:17], lw[0:48, 14:15],
                      lw[0:48, 15:16], op=OP.is_equal, deps=[t_jp])
        t_us = p.emit("vector", "tensor_single_scalar", lw[0:48, 17:18],
                      lw[0:48, 13:14], 0.5, op=OP.is_ge)
        vgap()
        t_wv = p.emit("vector", "tensor_tensor", lw[0:48, 18:19], lw[0:48, 16:17],
                      lw[0:48, 17:18], op=OP.mult)
        wvec = lw[0:48, 18:19]

        def ln_norm(src_ap, xm_ap, dump_ap, cb, deps):
            tv0 = p.emit("vector", "reduce_sum", st1[:, cb:cb + 1], src_ap,
                         axis=AX, deps=deps)
            ts0 = p.emit("scalar", "activation", st1[:, cb + 1:cb + 2],
                         st1[:, cb:cb + 1], AF.Copy, bias=0.0, scale=-INV_H,
                         deps=[tv0])
            tv1 = p.emit("vector", "tensor_scalar", xm_ap, src_ap,
                         st1[:, cb + 1:cb + 2], None, op0=OP.add,
                         deps=[ts0] + list(deps))
            ts1 = p.emit("scalar", "activation", dump_ap, xm_ap, AF.Square,
                         accum_out=st1[:, cb + 2:cb + 3], deps=[tv1])
            tv2 = p.emit("vector", "tensor_copy", st1[:, cb + 3:cb + 4],
                         st1[:, cb + 2:cb + 3], deps=[ts1])
            ts2 = p.emit("scalar", "activation", st1[:, cb + 4:cb + 5],
                         st1[:, cb + 3:cb + 4], AF.Sqrt, bias=EPS_AP,
                         scale=INV_H, deps=[tv2, t_eps])
            tv3 = p.emit("vector", "reciprocal", st1[:, cb + 5:cb + 6],
                         st1[:, cb + 4:cb + 5], deps=[ts2])
            ts3 = p.emit("scalar", "activation", xm_ap, xm_ap, AF.Copy,
                         bias=0.0, scale=st1[:, cb + 5:cb + 6], deps=[tv3])
            return ts3, ts1

        wslot_free = [None] * 4
        strip_free = [None] * 8
        qkv_done_tok = None
        last_ccin = None
        t_h1_prev = None
        t_tr_prev = None

        for half in range(2):
            prev_half_mm = qkv_done_tok
            ln_toks = []
            for lt in range(4):
                g_tt = half * 4 + lt
                c1 = C[:, 0:4096]
                c2 = C[:, 4096:8192]
                t_x = p.emit("gpsimd", "dma_start", inc=16, out=c1,
                             in_=x_own[g_tt * 128:(g_tt + 1) * 128, :],
                             deps=list(strip_free) + [prev_half_mm, t_h1_prev,
                                                      t_tr_prev])
                t_h1, _ = ln_norm(c1, c2, c1, 80, deps=[t_x])
                for d4 in range(8):
                    bank = PS[d4 % 8]
                    for c in range(4):
                        d = d4 * 4 + c
                        p.emit("tensor", "transpose", bank[:, c * 128:(c + 1) * 128],
                               c2[:, d * 128:(d + 1) * 128], ident[:],
                               deps=[t_h1, t_id] if (d4 == 0 and c == 0) else [t_h1])
                    t_tr = ("tensor", p.cnt["tensor"])
                    t_ev = p.emit("vector", "tensor_copy",
                                  A[:, lt * 4096 + d4 * 512: lt * 4096 + (d4 + 1) * 512],
                                  bank[:, 0:512], deps=[t_tr, prev_half_mm])
                    ln_toks.append(t_ev)
                t_h1_prev = t_h1
                t_tr_prev = ("tensor", p.cnt["tensor"])
            h1T_ready = ln_toks[-1]
            last_evac = [None] * 4
            last_tr_ev = [None] * 8
            for og in range(6):
                for d in range(32):
                    slot = (og * 32 + d) % 4
                    t_w = p.emit("sync", "dma_start", inc=16,
                                 out=wb[:, slot * 512:(slot + 1) * 512],
                                 in_=wqkv[d * 128:(d + 1) * 128, og * 512:(og + 1) * 512],
                                 deps=[wslot_free[slot]])
                    for lt in range(4):
                        deps = [t_w, h1T_ready]
                        if d == 0:
                            deps.append(last_evac[lt])
                        p.emit("tensor", "matmul", PS[lt][:, 0:512],
                               A[:, lt * 4096 + d * 128: lt * 4096 + (d + 1) * 128],
                               wb[:, slot * 512:(slot + 1) * 512],
                               start=(d == 0), stop=(d == 31), deps=deps)
                    wslot_free[slot] = ("tensor", p.cnt["tensor"])
                t_chain = ("tensor", p.cnt["tensor"])
                for lt in range(4):
                    g_tt = half * 4 + lt
                    sslot = (og * 4 + lt) % 8
                    sap = C[:, sslot * 512:(sslot + 1) * 512]
                    t_ev = p.emit("vector", "tensor_copy", sap, PS[lt][:, 0:512],
                                  deps=[t_chain, strip_free[sslot]])
                    last_evac[lt] = t_ev
                    if og < 4:
                        tb = 4 + lt
                        bank = PS[tb]
                        for c in range(4):
                            p.emit("tensor", "transpose", bank[:, c * 128:(c + 1) * 128],
                                   sap[:, c * 128:(c + 1) * 128], ident[:],
                                   deps=[t_ev, last_tr_ev[tb]] if c == 0 else [t_ev])
                        t_tr = ("tensor", p.cnt["tensor"])
                        if og < 2:
                            tk = None
                            for c in range(4):
                                dd = og * 4 + c
                                tk = p.emit("vector", "tensor_scalar",
                                            Bq[:, g_tt * 1024 + dd * 128: g_tt * 1024 + (dd + 1) * 128],
                                            bank[:, c * 128:(c + 1) * 128],
                                            bq_cm[:, dd:dd + 1], None, op0=OP.add,
                                            deps=[t_tr, t_bqcm])
                            last_tr_ev[tb] = tk
                            strip_free[sslot] = t_tr
                        else:
                            tdma = None
                            tk = None
                            for c in range(4):
                                kd = (og - 2) * 4 + c
                                tk = p.emit("vector", "tensor_scalar",
                                            sap[:, c * 128:(c + 1) * 128],
                                            bank[:, c * 128:(c + 1) * 128],
                                            bq_cm[:, 8 + kd:9 + kd], None, op0=OP.add,
                                            deps=[t_tr, t_bqcm])
                                tdma = p.emit("gpsimd", "dma_start", inc=16,
                                              out=cc_in_ap[kd * 128:(kd + 1) * 128,
                                                           g_tt * 128:(g_tt + 1) * 128],
                                              in_=sap[:, c * 128:(c + 1) * 128],
                                              deps=[tk])
                            last_tr_ev[tb] = tk
                            strip_free[sslot] = tdma
                            last_ccin = tdma
                    else:
                        tdma = p.emit("gpsimd", "dma_start", inc=16,
                                      out=cc_in_ap[D + g_tt * 128: D + (g_tt + 1) * 128,
                                                   (og - 4) * 512:(og - 3) * 512],
                                      in_=sap, deps=[t_ev])
                        strip_free[sslot] = tdma
                        last_ccin = tdma
            qkv_done_tok = ("tensor", p.cnt["tensor"])
        p.barrier()

        t_cc = p.emit("gpsimd", "collective_compute", "AllGather", OP.bypass,
                      replica_groups=GROUPS, ins=[cc_in_ap.opt()],
                      outs=[cc_out_ap.opt()], deps=[last_ccin])
        p.is_dma_tok["gpsimd"].add(t_cc[1])

        ktb = [A[:, 0:4096], A[:, 4096:8192]]
        vvb = [A[:, 8192:12288], A[:, 12288:16384]]
        pslot = [A[:, 16384 + q * 512: 16384 + (q + 1) * 512] for q in range(8)]
        ptslot = [A[:, 20480 + q * 512: 20480 + (q + 1) * 512] for q in range(8)]
        O = C
        t_o0 = p.emit("gpsimd", "memset", C[:], 0.0, deps=[t_cc])
        t_m0 = p.emit("gpsimd", "memset", st1[:, 0:8], 1e30)
        t_l0 = p.emit("gpsimd", "memset", st1[:, 8:16], 0.0)
        kv_used = [None, None]
        ps_free = [None] * 8
        pt_used = [None] * 8
        pt_evac = [None] * 8
        av_free = [None] * 8
        sc_done = [None] * 8
        c_read = [None] * 8
        l_upd = [None] * 8
        for kb in range(8):
            pp = kb % 2
            rb = (kb // 2) * 2048
            cb2 = (kb % 2) * 512
            tk_d = None
            for d in range(8):
                tk_d = p.emit("gpsimd", "dma_start", inc=16,
                              out=ktb[pp][:, d * 512:(d + 1) * 512],
                              in_=cc_out_ap[rb + d * 128: rb + (d + 1) * 128, cb2:cb2 + 512],
                              deps=[t_cc, kv_used[pp]] if d == 0 else [])
            tv_d = None
            for kc in range(4):
                tv_d = p.emit("gpsimd", "dma_start", inc=16,
                              out=vvb[pp][:, kc * 1024:(kc + 1) * 1024],
                              in_=cc_out_ap[rb + D + cb2 + kc * 128: rb + D + cb2 + (kc + 1) * 128, 0:D],
                              deps=[])
            t_s = [None] * 8
            for q in range(8):
                for d in range(8):
                    deps = [tk_d, ps_free[q]] if d == 0 else []
                    p.emit("tensor", "matmul", PS[q][:, 0:512],
                           Bq[:, q * 1024 + d * 128: q * 1024 + (d + 1) * 128],
                           ktb[pp][:, d * 512:(d + 1) * 512],
                           start=(d == 0), stop=(d == 7), deps=deps)
                t_s[q] = ("tensor", p.cnt["tensor"])
            s1 = [None] * 8
            for q in range(8):
                s1[q] = p.emit("vector", "reduce_max", st1[:, 16 + q:17 + q],
                               PS[q][:, 0:512], axis=AX, negate=True,
                               deps=[t_s[q], l_upd[q], sc_done[q]])
            s2 = [None] * 8
            for q in range(8):
                s2[q] = p.emit("scalar", "activation", st1[:, 24 + q:25 + q],
                               st1[:, 16 + q:17 + q], AF.Copy, bias=0.0,
                               scale=SM_SCALE, deps=[s1[q], sc_done[q]])
            s3 = [None] * 8
            for q in range(8):
                s3[q] = p.emit("vector", "tensor_tensor", st1[:, 32 + q:33 + q],
                               st1[:, 24 + q:25 + q], st1[:, q:q + 1], op=OP.min,
                               deps=[s2[q], t_m0, sc_done[q]])
            s4 = [None] * 8
            for q in range(8):
                s4[q] = p.emit("vector", "tensor_tensor", st1[:, 40 + q:41 + q],
                               st1[:, 24 + q:25 + q], st1[:, q:q + 1],
                               op=OP.subtract, deps=[s2[q]])
            for q in range(8):
                p.emit("vector", "tensor_single_scalar", st1[:, 48 + q:49 + q],
                       st1[:, 40 + q:41 + q], 0.0, op=OP.min)
            s5 = [None] * 8
            for q in range(8):
                s5[q] = p.emit("vector", "tensor_single_scalar", st1[:, 48 + q:49 + q],
                               st1[:, 48 + q:49 + q], -87.0, op=OP.max)
            s6 = [None] * 8
            for q in range(8):
                s6[q] = p.emit("scalar", "activation", st1[:, 64 + q:65 + q],
                               st1[:, 48 + q:49 + q], AF.Exp,
                               deps=[s5[q], c_read[q]])
            s7 = [None] * 8
            for q in range(8):
                s7[q] = p.emit("vector", "tensor_copy", st1[:, q:q + 1],
                               st1[:, 32 + q:33 + q], deps=[s6[q]])
            s8 = [None] * 8
            for q in range(8):
                s8[q] = p.emit("scalar", "activation", pslot[q], PS[q][:, 0:512],
                               AF.Exp, bias=st1[:, 32 + q:33 + q], scale=SM_SCALE,
                               accum_out=st1[:, 72 + q:73 + q],
                               deps=[s3[q], pt_used[q]])
                sc_done[q] = s8[q]
            for q in range(8):
                l_upd[q] = p.emit("vector", "scalar_tensor_tensor",
                                  st1[:, 8 + q:9 + q], st1[:, 8 + q:9 + q],
                                  st1[:, 64 + q:65 + q], st1[:, 72 + q:73 + q],
                                  op0=OP.mult, op1=OP.add,
                                  deps=[s6[q], s8[q], t_l0])
            for q in range(8):
                for c in range(4):
                    p.emit("tensor", "transpose", PS[q][:, c * 128:(c + 1) * 128],
                           pslot[q][:, c * 128:(c + 1) * 128], ident[:],
                           deps=[s8[q], pt_evac[q]] if c == 0 else [s8[q]])
                pt_used[q] = ("tensor", p.cnt["tensor"])
                t_pte = p.emit("vector", "tensor_copy", ptslot[q], PS[q][:, 0:512],
                               deps=[pt_used[q]])
                pt_evac[q] = t_pte
                ps_free[q] = t_pte
            for q in range(8):
                b0 = (2 * q) % 8
                for dh in range(2):
                    bank = PS[b0 + dh]
                    for kc in range(4):
                        deps = []
                        if kc == 0:
                            deps = [pt_evac[q], tv_d, av_free[b0 + dh],
                                    ps_free[b0 + dh]]
                        p.emit("tensor", "matmul", bank[:, 0:512],
                               ptslot[q][:, kc * 128:(kc + 1) * 128],
                               vvb[pp][:, kc * 1024 + dh * 512: kc * 1024 + dh * 512 + 512],
                               start=(kc == 0), stop=(kc == 3), deps=deps)
                    t_av = ("tensor", p.cnt["tensor"])
                    t_oe = p.emit("vector", "scalar_tensor_tensor",
                                  O[:, q * 1024 + dh * 512: q * 1024 + dh * 512 + 512],
                                  O[:, q * 1024 + dh * 512: q * 1024 + dh * 512 + 512],
                                  st1[:, 64 + q:65 + q], bank[:, 0:512],
                                  op0=OP.mult, op1=OP.add,
                                  deps=[t_av, t_o0, l_upd[q]])
                    av_free[b0 + dh] = t_oe
                    ps_free[b0 + dh] = t_oe
                    c_read[q] = t_oe
            kv_used[pp] = ("tensor", p.cnt["tensor"])
        p.barrier()

        rl_t = [None] * 8
        for q in range(8):
            rl_t[q] = p.emit("vector", "reciprocal", st1[:, 80 + q:81 + q],
                             st1[:, 8 + q:9 + q], deps=[l_upd[q]])
        on_t = [None] * 8
        for q in range(8):
            on_t[q] = p.emit("vector", "tensor_scalar", C[:, q * 1024:(q + 1) * 1024],
                             C[:, q * 1024:(q + 1) * 1024], st1[:, 80 + q:81 + q],
                             None, op0=OP.mult, deps=[rl_t[q]])
        ot_ev = [None] * 8
        for q in range(8):
            a = (q % 4) * 2
            for g in range(2):
                bank = PS[a + g]
                for c in range(4):
                    dd = g * 4 + c
                    p.emit("tensor", "transpose", bank[:, c * 128:(c + 1) * 128],
                           C[:, q * 1024 + dd * 128: q * 1024 + (dd + 1) * 128],
                           ident[:], deps=[on_t[q], ot_ev[a + g]] if c == 0 else [on_t[q]])
                t_tr = ("tensor", p.cnt["tensor"])
                tk = None
                for c in range(4):
                    dd = g * 4 + c
                    tk = p.emit("vector", "tensor_scalar",
                                Bq[:, dd * 1024 + q * 128: dd * 1024 + (q + 1) * 128],
                                bank[:, c * 128:(c + 1) * 128],
                                bq_cm[:, 16 + dd:17 + dd], None, op0=OP.add,
                                deps=[t_tr])
                ot_ev[a + g] = tk
        ot_done = ("vector", p.cnt["vector"])
        t_bo = p.emit("gpsimd", "dma_start", inc=16, out=C[0:1, 0:4096], in_=bo_r,
                      deps=[("tensor", p.cnt["tensor"])])
        wslot_free = [None] * 4
        z_ev = [None] * 8
        for og in range(8):
            for dd in range(8):
                slot = (og * 8 + dd) % 4
                t_w = p.emit("sync", "dma_start", inc=16,
                             out=wb[:, slot * 512:(slot + 1) * 512],
                             in_=wo[dd * 128:(dd + 1) * 128, og * 512:(og + 1) * 512],
                             deps=[wslot_free[slot]])
                if dd == 0:
                    for q in range(8):
                        p.emit("tensor", "matmul", PS[q][:, 0:512],
                               ones[0:1, 0:128], C[0:1, og * 512:(og + 1) * 512],
                               start=True, stop=False,
                               deps=[t_bo, z_ev[q], ot_done])
                for q in range(8):
                    p.emit("tensor", "matmul", PS[q][:, 0:512],
                           Bq[:, dd * 1024 + q * 128: dd * 1024 + (q + 1) * 128],
                           wb[:, slot * 512:(slot + 1) * 512],
                           start=False, stop=(dd == 7),
                           deps=[t_w] if q == 0 else [])
                wslot_free[slot] = ("tensor", p.cnt["tensor"])
            t_chain = ("tensor", p.cnt["tensor"])
            for q in range(8):
                z_ev[q] = p.emit("vector", "tensor_copy",
                                 A[:, q * 4096 + og * 512: q * 4096 + (og + 1) * 512],
                                 PS[q][:, 0:512], deps=[t_chain])
        p.barrier()

        x1_st = [None] * 8
        xs_free = [None, None]
        xm1_free = [None, None]
        h2_free = [None, None]
        dump_free = [None] * 8
        for tt in range(8):
            par = tt % 2
            xs = C[:, par * 4096:(par + 1) * 4096]
            xmz = Bq[:, par * 4096:(par + 1) * 4096]
            za = A[:, tt * 4096:(tt + 1) * 4096]
            cb = 80 + par * 6
            t_x = p.emit("gpsimd", "dma_start", inc=16, out=xs,
                         in_=x_own[tt * 128:(tt + 1) * 128, :],
                         deps=[xs_free[par], h2_free[par]])
            t_t1, _ = ln_norm(za, xmz, za, cb, deps=[xm1_free[par], dump_free[tt]])
            t_x1 = p.emit("vector", "tensor_tensor", xmz, xmz, xs,
                          op=OP.add, deps=[t_t1, t_x])
            x1_st[tt] = p.emit("gpsimd", "dma_start", inc=16,
                               out=x1_dram[tt * 128:(tt + 1) * 128, :],
                               in_=xmz, deps=[t_x1])
            t_h2, _ = ln_norm(xmz, xs, za, cb, deps=[t_x1])
            xm1_free[par] = x1_st[tt]
            xs_free[par] = t_h2
            for d4 in range(8):
                bank = PS[d4 % 8]
                for c in range(4):
                    d = d4 * 4 + c
                    p.emit("tensor", "transpose", bank[:, c * 128:(c + 1) * 128],
                           xs[:, d * 128:(d + 1) * 128], ident[:],
                           inc=1, deps=[t_h2])
                t_tr = ("tensor", p.cnt["tensor"])
                tv = p.emit("vector", "tensor_copy",
                            za[:, d4 * 512:(d4 + 1) * 512], bank[:, 0:512],
                            deps=[t_tr])
            h2_free[par] = ("tensor", p.cnt["tensor"])
            dump_free[tt] = None
        p.barrier()

        t_at = None
        for d in range(32):
            t_at = p.emit("sync", "dma_start", inc=16, out=Bq[:, d * 48:(d + 1) * 48],
                          in_=atc[d * 128:(d + 1) * 128, :])
        t_bt = p.emit("sync", "dma_start", inc=16, out=C[0:48, 0:4096], in_=btc)
        t_bf = p.emit("sync", "dma_start", inc=16, out=Bq[0:1, 3072:7168], in_=bffn_r)
        t_z1 = p.emit("gpsimd", "memset", st2[:, 0:8], 0.0)
        t_z2 = p.emit("gpsimd", "memset", st2[:, 8:16], 0.0)
        rt_ev = [None] * 8
        for tt in range(8):
            for d in range(32):
                p.emit("tensor", "matmul", PS[tt][0:48, 0:128],
                       Bq[:, d * 48:(d + 1) * 48],
                       A[:, tt * 4096 + d * 128: tt * 4096 + (d + 1) * 128],
                       start=(d == 0), stop=(d == 31),
                       deps=[t_at] if d == 0 else [])
            t_ch = ("tensor", p.cnt["tensor"])
            rt_ev[tt] = p.emit("vector", "tensor_scalar",
                               Bq[0:48, 1536 + tt * 128: 1536 + (tt + 1) * 128],
                               PS[tt][0:48, 0:128], wvec, None, op0=OP.mult,
                               deps=[t_ch, t_wv])
        wslot_free = [None] * 4
        y_ev = [None] * 8
        strip_dma = [None] * 8
        strip_sq = [None] * 8
        dump = Bq[:, 2560:3072]
        for og in range(8):
            for d in range(32):
                slot = (og * 32 + d) % 4
                t_w = p.emit("sync", "dma_start", inc=16,
                             out=wb[:, slot * 512:(slot + 1) * 512],
                             in_=wffnT[d * 128:(d + 1) * 128, og * 512:(og + 1) * 512],
                             deps=[wslot_free[slot], t_bf])
                if d == 0:
                    for tt in range(8):
                        p.emit("tensor", "matmul", PS[tt][:, 0:512],
                               ones[0:1, 0:128], Bq[0:1, 3072 + og * 512: 3072 + (og + 1) * 512],
                               start=True, stop=False,
                               deps=[t_bf, y_ev[tt], rt_ev[tt]])
                for tt in range(8):
                    p.emit("tensor", "matmul", PS[tt][:, 0:512],
                           A[:, tt * 4096 + d * 128: tt * 4096 + (d + 1) * 128],
                           wb[:, slot * 512:(slot + 1) * 512],
                           start=False, stop=False, deps=[t_w] if tt == 0 else [])
                wslot_free[slot] = ("tensor", p.cnt["tensor"])
                if d == 31:
                    for tt in range(8):
                        p.emit("tensor", "matmul", PS[tt][:, 0:512],
                               Bq[0:48, 1536 + tt * 128: 1536 + (tt + 1) * 128],
                               C[0:48, og * 512:(og + 1) * 512],
                               start=False, stop=True,
                               deps=[t_bt, rt_ev[tt]] if tt == 0 else [rt_ev[tt]])
            t_chain = ("tensor", p.cnt["tensor"])
            evs = []
            for tt in range(8):
                sap = C[:, 4096 + tt * 512: 4096 + (tt + 1) * 512]
                y_ev[tt] = p.emit("vector", "tensor_copy", sap, PS[tt][:, 0:512],
                                  deps=[t_chain, strip_dma[tt], strip_sq[tt]])
                evs.append(y_ev[tt])
            rs_t = [None] * 8
            for tt in range(8):
                rs_t[tt] = p.emit("vector", "reduce_sum", st2[:, 16 + tt:17 + tt],
                                  C[:, 4096 + tt * 512: 4096 + (tt + 1) * 512],
                                  axis=AX)
            for tt in range(8):
                strip_sq[tt] = p.emit("scalar", "activation", dump,
                                      C[:, 4096 + tt * 512: 4096 + (tt + 1) * 512],
                                      AF.Square, accum_out=st2[:, 24 + tt:25 + tt],
                                      deps=[evs[tt]])
            for tt in range(8):
                strip_dma[tt] = p.emit("gpsimd", "dma_start", inc=16,
                                       out=y_dram[tt * 128:(tt + 1) * 128,
                                                  og * 512:(og + 1) * 512],
                                       in_=C[:, 4096 + tt * 512: 4096 + (tt + 1) * 512],
                                       deps=[evs[tt]])
            for tt in range(8):
                p.emit("vector", "tensor_tensor", st2[:, tt:tt + 1],
                       st2[:, tt:tt + 1], st2[:, 16 + tt:17 + tt], op=OP.add,
                       deps=[t_z1, rs_t[tt]])
            for tt in range(8):
                p.emit("vector", "tensor_tensor", st2[:, 8 + tt:9 + tt],
                       st2[:, 8 + tt:9 + tt], st2[:, 24 + tt:25 + tt], op=OP.add,
                       deps=[t_z2, strip_sq[tt]])
        acc_done = ("vector", p.cnt["vector"])
        for tt in range(8):
            p.emit("vector", "tensor_single_scalar", st2[:, 32 + tt:33 + tt],
                   st2[:, tt:tt + 1], INV_H, op=OP.mult, deps=[acc_done])
        for tt in range(8):
            p.emit("vector", "tensor_single_scalar", st2[:, 40 + tt:41 + tt],
                   st2[:, 8 + tt:9 + tt], INV_H, op=OP.mult)
        for tt in range(8):
            p.emit("vector", "tensor_tensor", st2[:, 48 + tt:49 + tt],
                   st2[:, 32 + tt:33 + tt], st2[:, 32 + tt:33 + tt], op=OP.mult)
        v4_t = [None] * 8
        for tt in range(8):
            v4_t[tt] = p.emit("vector", "tensor_tensor", st2[:, 48 + tt:49 + tt],
                              st2[:, 40 + tt:41 + tt], st2[:, 48 + tt:49 + tt],
                              op=OP.subtract)
        sd_t = [None] * 8
        for tt in range(8):
            sd_t[tt] = p.emit("scalar", "activation", st2[:, 16 + tt:17 + tt],
                              st2[:, 48 + tt:49 + tt], AF.Sqrt, bias=EPS_AP,
                              scale=1.0, deps=[v4_t[tt], t_eps])
        for tt in range(8):
            p.emit("vector", "reciprocal", st2[:, 56 + tt:57 + tt],
                   st2[:, 16 + tt:17 + tt], deps=[sd_t[tt]])
        for tt in range(8):
            p.emit("vector", "tensor_single_scalar", st2[:, 24 + tt:25 + tt],
                   st2[:, 32 + tt:33 + tt], -1.0, op=OP.mult)
        p.barrier()

        y_free = [None, None]
        x1_free = [None, None]
        aa_free = [None, None]
        out_toks = []
        for tt in range(8):
            par = tt % 2
            c1 = C[:, par * 4096:(par + 1) * 4096]
            c2 = Bq[:, par * 4096:(par + 1) * 4096]
            aa = A[:, par * 4096:(par + 1) * 4096]
            t_y = p.emit("gpsimd", "dma_start", inc=16, out=c1,
                         in_=y_dram[tt * 128:(tt + 1) * 128, :],
                         deps=[y_free[par]])
            t_x1 = p.emit("sync", "dma_start", inc=16, out=c2,
                          in_=x1_dram[tt * 128:(tt + 1) * 128, :],
                          deps=[x1_free[par]])
            t_s1 = p.emit("vector", "tensor_scalar", aa, c1,
                          st2[:, 24 + tt:25 + tt], None, op0=OP.add,
                          deps=[t_y, aa_free[par]])
            y_free[par] = t_s1
            t_t = p.emit("scalar", "activation", aa, aa, AF.Copy,
                         bias=0.0, scale=st2[:, 56 + tt:57 + tt], deps=[t_s1])
            t_o = p.emit("vector", "tensor_tensor", aa, aa, c2,
                         op=OP.add, deps=[t_t, t_x1])
            x1_free[par] = t_o
            t_od = p.emit("gpsimd", "dma_start", inc=16,
                          out=out_own[tt * 128:(tt + 1) * 128, :],
                          in_=aa, deps=[t_o])
            out_toks.append(t_od)
            aa_free[par] = t_od
        final = p.cnt["gpsimd"]
        p.ops["gpsimd"].append(("w", [("gpsimd", final)]))

        @block.tensor
        def _(eng):
            p.replay("tensor", eng, sems)

        @block.vector
        def _(eng):
            p.replay("vector", eng, sems)

        @block.scalar
        def _(eng):
            p.replay("scalar", eng, sems)

        @block.gpsimd
        def _(eng):
            p.replay("gpsimd", eng, sems)

        @block.sync
        def _(eng):
            p.replay("sync", eng, sems)

    return nc


def kernel(L0_attn_x, L0_attn_Wqkv, L0_attn_bqkv, L0_attn_Wo, L0_attn_bo,
           L0_ffn_weight, L0_ffn_bias, L0_ffn_lora_A, L0_ffn_lora_B,
           L0_ffn_lora_start_indices):
    f = np.float32
    x = np.asarray(L0_attn_x, f)
    wqkv = np.ascontiguousarray(np.asarray(L0_attn_Wqkv, f))
    bqkv_cm = np.ascontiguousarray(np.asarray(L0_attn_bqkv, f).reshape(24, 128).T)
    wo = np.ascontiguousarray(np.asarray(L0_attn_Wo, f))
    bo_r = np.ascontiguousarray(np.asarray(L0_attn_bo, f).reshape(1, H))
    wffnT = np.ascontiguousarray(np.asarray(L0_ffn_weight, f).T)
    bffn_r = np.ascontiguousarray(np.asarray(L0_ffn_bias, f).reshape(1, H))
    atc = np.ascontiguousarray(np.asarray(L0_ffn_lora_A, f).transpose(2, 0, 1).reshape(H, 48))
    btc = np.ascontiguousarray(np.asarray(L0_ffn_lora_B, f).transpose(0, 2, 1).reshape(48, H))
    idx = np.ascontiguousarray(np.asarray(L0_ffn_lora_start_indices, np.int32).reshape(1, 4))

    nc = build()
    in_maps = []
    for c in range(8):
        b, qc = c // 4, c % 4
        in_maps.append({
            "x_own": np.ascontiguousarray(x[b, qc * T:(qc + 1) * T, :]),
            "wqkv": wqkv, "bqkv_cm": bqkv_cm, "wo": wo, "bo_r": bo_r,
            "wffnT": wffnT, "bffn_r": bffn_r, "atc": atc, "btc": btc,
            "idx_in": idx, "bval_in": np.array([[float(b)]], f),
        })
    res = run_bass_kernel_spmd(nc, in_maps, core_ids=list(range(8)))
    out = np.empty((2, 4096, H), f)
    for c in range(8):
        b, qc = c // 4, c % 4
        out[b, qc * T:(qc + 1) * T, :] = np.asarray(res.results[c]["out_own"])
    return out
